# version 04
# baseline (speedup 1.0000x reference)
import numpy as np
from contextlib import ExitStack

import concourse.bass as bass
import concourse.mybir as mybir
from concourse.bass_utils import run_bass_kernel_spmd

F32 = mybir.dt.float32
BF16 = mybir.dt.bfloat16
AF = mybir.ActivationFunctionType
ALU = mybir.AluOpType
AX = mybir.AxisListType

T = 2048
D = 2048
H = 16
HD = 128
NE = 8
FF = 2048
NT = 16
ND = 16
EPS = 1e-5
INVSQD = float(1.0 / np.sqrt(2048.0))
NHALF = 2
TSH = 8
NPATH = NE + 1


class Plan:
    def __init__(self):
        self.cnt = {}
        self.ev = {}
        self.recording = True

    def inc(self, sem, n, tag):
        self.cnt[sem] = self.cnt.get(sem, 0) + n
        if tag is not None:
            assert tag not in self.ev, f"dup event {tag}"
            self.ev[tag] = (sem, self.cnt[sem])


class Eng:
    def __init__(self, plan, sems, real=None):
        self.plan = plan
        self.sems = sems
        self.real = real
        self._last = None

    def __getattr__(self, op):
        def call(*a, **kw):
            if self.plan.recording:
                self._last = None
                return self
            self._last = getattr(self.real, op)(*a, **kw)
            return self
        return call

    def inc(self, sem, n=1, tag=None):
        if self.plan.recording:
            self.plan.inc(sem, n, tag)
        else:
            self._last.then_inc(self.sems[sem], n)
        return self

    def wait(self, tag):
        if not self.plan.recording:
            sem, val = self.plan.ev[tag]
            self.real.wait_ge(self.sems[sem], val)
        return self

    def wait_val(self, sem, val):
        if not self.plan.recording:
            self.real.wait_ge(self.sems[sem], val)
        return self


def build_nc(dbg=()):
    nc = bass.Bass("TRN2", target_bir_lowering=False, debug=False)
    st = ExitStack()

    def dram(name, shape, dtype):
        if name in dbg:
            if dtype == BF16:
                return nc.dram_tensor(name, shape, mybir.dt.uint16,
                                      kind="ExternalOutput").ap().bitcast(BF16)
            return nc.dram_tensor(name, shape, dtype, kind="ExternalOutput").ap()
        return nc.dram_tensor(name, shape, dtype, kind="Internal").ap()

    x_d = nc.dram_tensor("x", [T, D], F32, kind="ExternalInput").ap()
    lng_d = nc.dram_tensor("ln_g", [D], F32, kind="ExternalInput").ap()
    lnb_d = nc.dram_tensor("ln_b", [D], F32, kind="ExternalInput").ap()
    w_qkv_d = [nc.dram_tensor(n, [D, D], F32, kind="ExternalInput").ap()
               for n in ["wq", "wk", "wv"]]
    wo_d = nc.dram_tensor("wo", [D, D], F32, kind="ExternalInput").ap()
    b_qkvo_d = [nc.dram_tensor(n, [D], F32, kind="ExternalInput").ap()
                for n in ["bq", "bk", "bv", "bo"]]
    rw_d = nc.dram_tensor("rw", [D, NE], F32, kind="ExternalInput").ap()
    gw_d = nc.dram_tensor("gw", [NE, D, FF], F32, kind="ExternalInput").ap()
    uw_d = nc.dram_tensor("uw", [NE, D, FF], F32, kind="ExternalInput").ap()
    dw_d = nc.dram_tensor("dw", [NE, FF, D], F32, kind="ExternalInput").ap()
    sg_d = nc.dram_tensor("sgw", [D, FF], F32, kind="ExternalInput").ap()
    su_d = nc.dram_tensor("suw", [D, FF], F32, kind="ExternalInput").ap()
    sd_d = nc.dram_tensor("sdw", [FF, D], F32, kind="ExternalInput").ap()
    out_d = nc.dram_tensor("out", [T, D], F32, kind="ExternalOutput").ap()

    xn_d = dram("xn_s", [T, D], BF16)
    q_d = dram("q_s", [T, D], BF16)
    k_d = dram("k_s", [T, D], BF16)
    v_d = dram("v_s", [T, D], BF16)
    ctx_d = dram("ctx_s", [T, D], BF16)
    x1_d = dram("x1_s", [T, D], F32)
    x1n_d = dram("x1n_s", [T, D], BF16)
    h_d = dram("h_s", [T // 2, FF], BF16)
    y_d = dram("y_s", [D, T // 2], BF16)

    SB = lambda *a, **kw: st.enter_context(nc.sbuf_tensor(*a, **kw))[:]
    lng_cm = SB("lng_cm", [128, ND], F32)
    lnb_cm = SB("lnb_cm", [128, ND], F32)
    bo_b = SB("bo_b", [128, D], BF16)
    brow = SB("brow", [1, 512], F32)
    ones1 = SB("ones1", [1, 128], F32)
    rw_s = SB("rw_s", [128, ND, NE], BF16)
    alpha = SB("alpha", [128, TSH, NPATH], F32)
    stat4 = SB("stat4", [128, 2, 4, 6], F32)
    mv2 = SB("mv2", [128, 2, 2], F32)
    rho = SB("rho", [128, 2, 1], F32)
    nmr = SB("nmr", [128, 2, 1], F32)
    t128 = SB("t128", [128, 2, 1], F32)
    lgts = SB("lgts", [128, TSH, NE], F32)
    lmax = SB("lmax", [128, TSH, 1], F32)
    sigm = SB("sigm", [128, TSH, 1], F32)
    msk8 = SB("msk8", [128, TSH, NE], F32)
    smx = SB("smx", [128, H], F32)

    MB = 1024 * 1024 // 128
    A8 = SB("A8", [128, 8 * MB], mybir.dt.uint8, align_bytes=512)
    B12 = SB("B12", [128, 12 * MB], mybir.dt.uint8, align_bytes=512)
    C2 = SB("C2", [128, 7 * MB // 2], mybir.dt.uint8, align_bytes=512)
    PS = st.enter_context(nc.psum_tensor("psall", [128, 4096], F32))[:]

    def carve(arena, off, shape, dtype):
        assert shape[0] == 128
        nby = int(np.prod(shape[1:])) * mybir.dt.size(dtype)
        v = arena[:, off:off + nby].bitcast(dtype)
        if len(shape) == 3:
            v = v.rearrange("p (a b) -> p a b", b=shape[2])
        elif len(shape) == 4:
            v = v.rearrange("p (a b c) -> p a b c", b=shape[2], c=shape[3])
        elif len(shape) == 5:
            v = v.rearrange("p (a b c d) -> p a b c d",
                            b=shape[2], c=shape[3], d=shape[4])
        return v, off + nby

    xnbuf, _ = carve(A8, 0, [128, 2, D], BF16)
    xnT, _ = carve(A8, 0, [128, ND, T], BF16)
    gus, _ = carve(A8, 0, [128, 2, 2, ND, 512], BF16)
    wqkv, _ = carve(B12, 0, [128, 2, 3, ND, 256], BF16)
    qt, o = carve(B12, 0, [128, 2, D], BF16)
    kt, o = carve(B12, o, [128, 2, D], BF16)
    vt, o = carve(B12, o, [128, 2, D], BF16)
    prodb, o = carve(B12, o, [128, 2, H, HD], BF16)
    scor, o = carve(B12, o, [128, 2, H, H], F32)
    wsm, o = carve(B12, o, [128, 2, H, H], F32)
    ctxf, o = carve(B12, o, [128, 2, H, HD], F32)
    ctxb, o = carve(B12, o, [128, 2, D], BF16)
    bqkv_b, o = carve(B12, o, [128, 3, D], BF16)
    ctxT, o = carve(B12, 0, [128, ND, T], BF16)
    wo_s, o = carve(B12, o, [128, 4, ND, 256], BF16)
    x1rw, _ = carve(B12, 0, [128, 2, D], F32)
    ycm, o = carve(B12, 0, [128, ND, T // 2], BF16)
    x1nT, o6 = carve(B12, o, [128, ND, T // 2], BF16)
    HT, _ = carve(B12, o6, [128, 16, T // 2], BF16)
    xtl, o7 = carve(B12, o6, [128, 2, D], F32)
    ytl, _ = carve(B12, o7, [128, 2, D], BF16)
    xbuf, _ = carve(C2, 0, [128, 2, D], F32)
    qkve, _ = carve(C2, 0, [128, 2, 3, 256], BF16)
    xrl, o = carve(C2, 0, [128, 4, 256], F32)
    x1st, o = carve(C2, o, [128, 4, 256], F32)
    x1nb, o = carve(C2, o, [128, 2, D], BF16)
    hst, o = carve(C2, 0, [128, 2, 4, 512], BF16)
    sgb, o = carve(C2, o, [128, 2, 512], F32)
    dsl, o = carve(C2, o, [128, 4, 16, 128], BF16)

    plan = Plan()

    class SemDict(dict):
        def __missing__(self, n):
            h = st.enter_context(nc.semaphore(n))
            self[n] = h
            return h
    sems = SemDict()

    def u_dec(u):
        return u // 32, (u // 8) % 4, u % 8

    def l_dec(l):
        return l // (NPATH * 4), (l // 4) % NPATH, l % 4

    def d_dec(l):
        return l // (NPATH * ND), (l // ND) % NPATH, l % ND

    def emit_sync(s):
        for bi in range(4):
            for c in range(4):
                if (bi, c) > (0, 0):
                    pc = bi * 4 + c - 1
                    s.wait(("p0_bias_ps", (pc // 4, pc % 4)))
                s.dma_start(brow, b_qkvo_d[bi][c * 512:(c + 1) * 512].unsqueeze(0)
                            ).inc("s_brow", 16, ("p0_brow", (bi, c)))
        for i in range(NT):
            if i >= 2:
                s.wait(("p1_xn", i - 2))
            s.dma_start(xbuf[:, i % 2, :], x_d[i * 128:(i + 1) * 128, :]).inc(f"s_x{i % 2}", 16, ("p1_x", i))
        for i in range(NT):
            if i < 2:
                s.wait(("p2_mm", (7, NT - 1)))
            else:
                s.wait(("p3_c", (i - 2, H - 1, H - 1)))
            s.wait(("p2_store", (7, i, 2)))
            s.dma_start(qt[:, i % 2, :], q_d[i * 128:(i + 1) * 128, :]).inc(f"s_qkv{i % 2}", 16)
            s.dma_start(kt[:, i % 2, :], k_d[i * 128:(i + 1) * 128, :]).inc(f"s_qkv{i % 2}", 16)
            s.dma_start(vt[:, i % 2, :], v_d[i * 128:(i + 1) * 128, :]).inc(f"s_qkv{i % 2}", 16, ("p3_qkv", i))
        for p in range(2):
            for ts in range(NT):
                for jq in range(4):
                    idx = p * 64 + ts * 4 + jq
                    q8 = p * 4 + jq
                    if idx < 4:
                        s.wait(("p1_xn", NT - 1))
                    else:
                        s.wait(("p4_evac", idx - 4))
                    s.dma_start(xrl[:, idx % 4, :],
                                x_d[ts * 128:(ts + 1) * 128, q8 * 256:(q8 + 1) * 256]
                                ).inc(f"s_xrl{idx % 4}", 16, ("p4_x", idx))
        for i in range(NT):
            if i >= 2:
                s.wait(("p5_xn", i - 2))
            if i == 0:
                s.wait(("p4_mm", 8 * NT - 1))
            s.wait(("p4_x1sp", 64 + i * 4 + 3))
            s.dma_start(x1rw[:, i % 2, :], x1_d[i * 128:(i + 1) * 128, :]).inc(f"s_x1rw{i % 2}", 16, ("p5_x1", i))
        for hf in range(NHALF):
            for j in range(TSH):
                jj = hf * TSH + j
                r = hf * 1024 + j * 128
                if j == 0:
                    s.wait(("p6_y_ps", (hf * NPATH + NPATH) * ND - 1))
                if jj >= 2:
                    s.wait(("p7_out", jj - 2))
                s.dma_start(xtl[:, j % 2, :], x1_d[r:r + 128, :]).inc(f"s_xtl{j % 2}", 16, ("p7_x1", jj))
                s.wait(("p7_add", jj))
                s.dma_start(out_d[r:r + 128, :], xtl[:, j % 2, :]).inc(f"s_out{j % 2}", 16, ("p7_out", jj))
        s.wait(("p7_out", NHALF * TSH - 2))
        s.wait(("p7_out", NHALF * TSH - 1))

    def emit_gpsimd(g):
        g.dma_start(lng_cm, lng_d.rearrange("(k p) -> p k", p=128),
                    allow_slow_non_contiguous=True).inc("g_small", 16)
        g.dma_start(lnb_cm, lnb_d.rearrange("(k p) -> p k", p=128),
                    allow_slow_non_contiguous=True).inc("g_small", 16, "p0_ln")
        g.dma_start(rw_s, rw_d.rearrange("(k p) e -> p k e", p=128),
                    allow_slow_non_contiguous=True).inc("g_small", 16, "p0_rw")
        for q8 in range(8):
            for m in range(3):
                if q8 >= 2 and m == 0:
                    g.wait(("p2_mm", (q8 - 2, NT - 1)))
                g.dma_start(wqkv[:, q8 % 2, m, :, :],
                            w_qkv_d[m].rearrange("(k p) n -> p k n", p=128)
                            [:, :, q8 * 256:(q8 + 1) * 256]
                            ).inc(f"g_wqkv{q8 % 2}", 16, ("p2_slab", (q8, m)))
        for p in range(2):
            for jq in range(4):
                q8 = p * 4 + jq
                if p == 0 and jq == 0:
                    g.wait(("p2_evac", (7, NT - 1, 2)))
                if p == 1:
                    g.wait(("p4_mm", 63))
                g.dma_start(wo_s[:, jq, :, :],
                            wo_d.rearrange("(k p) n -> p k n", p=128)
                            [:, :, q8 * 256:(q8 + 1) * 256]).inc(f"g_wo{jq}", 16, ("p4_slab", (p, jq)))
        NPTOT = NHALF * NPATH

        def wdefs(path):
            hf, e = divmod(path, NPATH)
            ge = gw_d[e] if e < NE else sg_d
            ue = uw_d[e] if e < NE else su_d
            de = dw_d[e] if e < NE else sd_d
            return hf, e, ge, ue, de

        def emit_gus(path, f4):
            hf, e, ge, ue, de = wdefs(path)
            l = path * 4 + f4
            if l < 2:
                g.wait(("p2_mm", (7, NT - 1)))
            else:
                g.wait(("p6_gu_ps", (l - 2) * 8 + 7))
            g.dma_start(gus[:, l % 2, 0, :, :],
                        ge.rearrange("(k p) f -> p k f", p=128)
                        [:, :, f4 * 512:(f4 + 1) * 512]).inc(f"g_gus{l % 2}", 16)
            g.dma_start(gus[:, l % 2, 1, :, :],
                        ue.rearrange("(k p) f -> p k f", p=128)
                        [:, :, f4 * 512:(f4 + 1) * 512]).inc(f"g_gus{l % 2}", 16, ("p6_gu", l))

        def emit_dsl(path, dt):
            hf, e, ge, ue, de = wdefs(path)
            l = path * ND + dt
            if l < 4:
                g.wait(("p5_sp", NT - 1))
            else:
                g.wait(("p6_y_ps", l - 4))
            g.dma_start(dsl[:, l % 4, :, :],
                        de.rearrange("(k p) d -> p k d", p=128)
                        [:, :, dt * 128:(dt + 1) * 128]).inc(f"g_dsl{l % 4}", 16, ("p6_dsl", l))

        emit_gus(0, 0)
        emit_gus(0, 1)
        emit_gus(0, 2)
        emit_gus(0, 3)
        for path in range(NPTOT):
            for dt in range(4):
                emit_dsl(path, dt)
            if path + 1 < NPTOT:
                emit_gus(path + 1, 0)
            for dt in range(4, ND):
                emit_dsl(path, dt)
            if path + 1 < NPTOT:
                emit_gus(path + 1, 1)
                emit_gus(path + 1, 2)
                emit_gus(path + 1, 3)

    def emit_scalar(a):
        for i in range(NT):
            b = i % 2
            a.wait(("p1_t128", i))
            a.activation(rho[:, b, :], t128[:, b, :], AF.Ln).inc("aq", 1, ("p1_lnx", i))
            a.wait(("p1_lnx", i))
            a.activation(rho[:, b, :], rho[:, b, :], AF.Exp, scale=-0.5).inc("aq", 1, ("p1_rho", i))
            a.wait(("p1_xn", i))
            a.dma_start(xn_d[i * 128:(i + 1) * 128, :], xnbuf[:, b, :]).inc(f"a_xnsp{b}", 16, ("p1_sp", i))
        a.wait(("p1_sp", NT - 1))
        for kb in range(ND):
            a.dma_start_transpose(xnT[:, kb, :],
                                  xn_d[:, kb * 128:(kb + 1) * 128]).inc("a_xnT", 16, ("p2_xnT", kb))
        for q8 in range(8):
            for ts in range(NT):
                for m, dd in [(0, q_d), (1, k_d), (2, v_d)]:
                    a.wait(("p2_evac", (q8, ts, m)))
                    a.dma_start(dd[ts * 128:(ts + 1) * 128, q8 * 256:(q8 + 1) * 256],
                                qkve[:, ts % 2, m, :]).inc(f"a_qkvsp{ts % 2}", 16, ("p2_store", (q8, ts, m)))
        for i in range(NT):
            b = i % 2
            if i >= 2:
                a.wait(("p3_c", (i - 2, H - 1, H - 1)))
            a.wait(("p3_smx", i))
            a.activation(wsm[:, b, :, :], scor[:, b, :, :], AF.Exp,
                         scale=INVSQD).inc("aq", 1, ("p3_exp", i))
            a.wait(("p3_c", (i, H - 1, H - 1)))
            a.dma_start(ctx_d[i * 128:(i + 1) * 128, :], ctxb[:, b, :]).inc(f"a_ctxsp{b}", 16, ("p3_ctxsp", i))
        a.wait(("p3_ctxsp", NT - 1))
        for kb in range(ND):
            a.dma_start_transpose(ctxT[:, kb, :],
                                  ctx_d[:, kb * 128:(kb + 1) * 128]).inc("a_ctxT", 16, ("p4_ctxT", kb))
        for p in range(2):
            for ts in range(NT):
                for jq in range(4):
                    idx = p * 64 + ts * 4 + jq
                    q8 = p * 4 + jq
                    a.wait(("p4_evac", idx))
                    a.dma_start(x1_d[ts * 128:(ts + 1) * 128, q8 * 256:(q8 + 1) * 256],
                                x1st[:, idx % 4, :]).inc(f"a_x1sp{idx % 4}", 16, ("p4_x1sp", idx))
        for i in range(NT):
            b = i % 2
            a.wait(("p5_t128", i))
            a.activation(rho[:, b, :], t128[:, b, :], AF.Ln).inc("aq", 1, ("p5_lnx", i))
            a.wait(("p5_lnx", i))
            a.activation(rho[:, b, :], rho[:, b, :], AF.Exp, scale=-0.5).inc("aq", 1, ("p5_rho", i))
            a.wait(("p5_xn", i))
            a.dma_start(x1n_d[i * 128:(i + 1) * 128, :], x1nb[:, b, :]).inc(f"a_x1nsp{b}", 16, ("p5_sp", i))
        for hf in range(NHALF):
            for kb in range(ND):
                if kb == 0:
                    a.wait(("p5_sp", NT - 1))
                a.dma_start_transpose(x1nT[:, kb, :],
                                      x1n_d[hf * 1024:(hf + 1) * 1024, kb * 128:(kb + 1) * 128]
                                      ).inc("a_x1nT", 16, ("p6_x1nT", (hf, kb)))
            for ts in range(TSH):
                a.wait(("p6_lmax", (hf, ts)))
                a.activation(sigm[:, ts, :], lmax[:, ts, :], AF.Exp,
                             scale=-1.0).inc("aq", 1, ("p6_sigm", (hf, ts)))
            a.wait(("p6_al", (hf, TSH - 1)))
            a.wait(("p6_a8", hf))
            for e in range(NPATH):
                for f4 in range(4):
                    l = (hf * NPATH + e) * 4 + f4
                    for ts in range(TSH):
                        u = (hf * NPATH + e) * 32 + f4 * 8 + ts
                        b = u % 2
                        a.wait(("p6_gu_ps", u))
                        if u >= 2:
                            a.wait(("p6_h", u - 2))
                        a.activation(sgb[:, b, :], PS[:, (b * 2) * 512:(b * 2) * 512 + 512],
                                     AF.Silu,
                                     scale=alpha[:, ts, e:e + 1]).inc("aq", 1, ("p6_sg", u))
                    for jh in range(2):
                        sid = l * 2 + jh
                        a.wait(("p6_h", l * 8 + jh * 4 + 3))
                        a.dma_start(h_d.rearrange("(s p) f -> p s f", p=128)
                                    [:, jh * 4:jh * 4 + 4, f4 * 512:(f4 + 1) * 512],
                                    hst[:, sid % 2, :, :]).inc(f"a_hsp{sid % 2}", 16, ("p6_hsp", sid))
                    for fb in range(f4 * 4, f4 * 4 + 4):
                        if fb == f4 * 4:
                            le = hf * NPATH + e
                            if le >= 1:
                                a.wait(("p6_y_ps", le * ND - 1))
                            if hf == 1 and e == 0 and f4 == 0:
                                a.wait(("p7_add", TSH - 1))
                        a.wait(("p6_hsp", l * 2 + 1))
                        a.dma_start_transpose(HT[:, fb, :],
                                              h_d[:, fb * 128:(fb + 1) * 128]
                                              ).inc(f"a_HT{fb % 8}", 16, ("p6_HT", (hf, e, fb)))
            a.wait(("p6_ylast", hf))
            a.dma_start(y_d.rearrange("(k p) t -> p k t", p=128), ycm).inc("a_ysp", 16, ("p6_ysp", hf))
            for j in range(TSH):
                jj = hf * TSH + j
                if j == 0:
                    a.wait(("p6_y_ps", (hf * NPATH + NPATH) * ND - 1))
                a.wait(("p6_ysp", hf))
                if jj >= 2:
                    a.wait(("p7_add", jj - 2))
                a.dma_start_transpose(ytl[:, j % 2, :],
                                      y_d[:, j * 128:(j + 1) * 128]
                                      ).inc(f"a_ytl{j % 2}", 16, ("p7_y", jj))

    def emit_vector(v):
        v.memset(ones1, 1.0).inc("vq", 1, "p0_ones")
        for bi in range(4):
            for c in range(4):
                v.wait(("p0_bias_ps", (bi, c)))
                dstb = bqkv_b[:, bi, :] if bi < 3 else bo_b
                v.tensor_copy(dstb[:, c * 512:(c + 1) * 512],
                              PS[:, c * 512:(c + 1) * 512]).inc("vq", 1, ("p0_bias", (bi, c)))
        for i in range(NT):
            b = i % 2
            v.wait(("p1_x", i))
            if i >= 2:
                v.wait(("p1_xn", i - 2))
                v.wait(("p1_sp", i - 2))
            for c in range(4):
                v.bn_stats(stat4[:, b, c, :], xbuf[:, b, c * 512:(c + 1) * 512]).inc(
                    "vq", 1, ("p1_bn", (i, c)))
            v.wait(("p1_bn", (i, 3)))
            v.bn_aggr(mv2[:, b, :], stat4[:, b, :, :]).inc("vq", 1, ("p1_mv", i))
            v.wait(("p1_mv", i))
            v.tensor_scalar(t128[:, b, :], mv2[:, b, 1:2], 1.0 + EPS, EPS * EPS,
                            ALU.mult, ALU.add).inc("vq", 1, ("p1_t128", i))
            v.wait(("p1_rho", i))
            v.tensor_scalar(nmr[:, b, :], mv2[:, b, 0:1], rho[:, b, 0:1], -1.0,
                            ALU.mult, ALU.mult).inc("vq", 1, ("p1_nmr", i))
            v.wait(("p1_nmr", i))
            v.tensor_scalar(xnbuf[:, b, :], xbuf[:, b, :], rho[:, b, 0:1],
                            nmr[:, b, 0:1], ALU.mult, ALU.add).inc("vq", 1, ("p1_xn", i))
        for kb in range(ND):
            v.wait(("p2_xnT", ND - 1))
            if kb == 0:
                v.wait("p0_ln")
            v.tensor_scalar(xnT[:, kb, :], xnT[:, kb, :], lng_cm[:, kb:kb + 1],
                            lnb_cm[:, kb:kb + 1], ALU.mult, ALU.add).inc("vq", 1, ("p2_aff", kb))
        for q8 in range(8):
            for ts in range(NT):
                b = ts % 2
                for m in range(3):
                    v.wait(("p2_mm", (q8, ts)))
                    idx = q8 * NT + ts
                    if idx >= 2:
                        v.wait(("p2_store", ((idx - 2) // NT, (idx - 2) % NT, m)))
                    v.tensor_tensor(qkve[:, b, m, :],
                                    PS[:, (m * 2 + b) * 512:(m * 2 + b) * 512 + 256],
                                    bqkv_b[:, m, q8 * 256:(q8 + 1) * 256],
                                    ALU.add).inc("vq", 1, ("p2_evac", (q8, ts, m)))
        for i in range(NT):
            b = i % 2
            v.wait(("p3_qkv", i))
            if i >= 2:
                v.wait(("p3_exp", i - 2))
            for h in range(H):
                pb = h % 2
                if h >= 2:
                    v.wait(("p3_red", (i, h - 2)))
                qv = qt[:, b, h * 128:(h + 1) * 128].unsqueeze(1).broadcast_to([128, H, HD])
                kv = kt[:, b, :].rearrange("p (g d) -> p g d", d=HD)
                v.tensor_tensor(prodb[:, pb, :, :], qv, kv, ALU.mult).inc("vq", 1, ("p3_prod", (i, h)))
                v.wait(("p3_prod", (i, h)))
                v.tensor_reduce(scor[:, b, h, :], prodb[:, pb, :, :], AX.X, ALU.add).inc(
                    "vq", 1, ("p3_red", (i, h)))
            v.wait(("p3_red", (i, H - 1)))
            v.tensor_reduce(smx, scor[:, b, :, :], AX.X, ALU.max).inc("vq", 1, ("p3_mx", i))
            v.wait(("p3_mx", i))
            v.tensor_tensor(scor[:, b, :, :], scor[:, b, :, :],
                            smx.unsqueeze(2).broadcast_to([128, H, H]),
                            ALU.subtract).inc("vq", 1, ("p3_smx", i))
            v.wait(("p3_exp", i))
            v.tensor_reduce(smx, wsm[:, b, :, :], AX.X, ALU.add).inc("vq", 1, ("p3_sum", i))
            v.wait(("p3_sum", i))
            v.reciprocal(smx, smx).inc("vq", 1, ("p3_rec", i))
            v.wait(("p3_rec", i))
            v.tensor_tensor(wsm[:, b, :, :], wsm[:, b, :, :],
                            smx.unsqueeze(2).broadcast_to([128, H, H]),
                            ALU.mult).inc("vq", 1, ("p3_w", i))
            v.wait(("p3_w", i))
            for gg in range(H):
                for h in range(H):
                    if gg == H - 1:
                        dst = ctxb[:, b, h * 128:(h + 1) * 128]
                    else:
                        dst = ctxf[:, b, h, :]
                    if gg == 15 and h == 0 and i >= 2:
                        v.wait(("p3_ctxsp", i - 2))
                    if gg == 0:
                        v.tensor_scalar(dst, vt[:, b, 0:128],
                                        wsm[:, b, h, 0:1], None, ALU.mult).inc(
                            "vq", 1, ("p3_c", (i, gg, h)))
                    else:
                        v.wait(("p3_c", (i, gg - 1, h)))
                        v.scalar_tensor_tensor(dst, vt[:, b, gg * 128:(gg + 1) * 128],
                                               wsm[:, b, h, gg:gg + 1], ctxf[:, b, h, :],
                                               ALU.mult, ALU.add).inc("vq", 1, ("p3_c", (i, gg, h)))
        for p in range(2):
            for ts in range(NT):
                for jq in range(4):
                    idx = p * 64 + ts * 4 + jq
                    q8 = p * 4 + jq
                    bb = idx % 4
                    v.wait(("p4_mm", idx))
                    v.wait(("p4_x", idx))
                    if idx >= 4:
                        v.wait(("p4_x1sp", idx - 4))
                    v.tensor_tensor(x1st[:, bb, :], PS[:, bb * 512:bb * 512 + 256],
                                    bo_b[:, q8 * 256:(q8 + 1) * 256], ALU.add).inc(
                        "vq", 1, ("p4_bo", idx))
                    v.wait(("p4_bo", idx))
                    v.tensor_tensor(x1st[:, bb, :], x1st[:, bb, :], xrl[:, bb, :],
                                    ALU.add).inc("vq", 1, ("p4_evac", idx))
        for i in range(NT):
            b = i % 2
            v.wait(("p5_x1", i))
            if i >= 2:
                v.wait(("p5_xn", i - 2))
                v.wait(("p5_sp", i - 2))
            for c in range(4):
                v.bn_stats(stat4[:, b, c, :], x1rw[:, b, c * 512:(c + 1) * 512]).inc(
                    "vq", 1, ("p5_bn", (i, c)))
            v.wait(("p5_bn", (i, 3)))
            v.bn_aggr(mv2[:, b, :], stat4[:, b, :, :]).inc("vq", 1, ("p5_mv", i))
            v.wait(("p5_mv", i))
            v.tensor_scalar(t128[:, b, :], mv2[:, b, 1:2], 1.0, EPS,
                            ALU.mult, ALU.add).inc("vq", 1, ("p5_t128", i))
            v.wait(("p5_rho", i))
            v.tensor_scalar(nmr[:, b, :], mv2[:, b, 0:1], rho[:, b, 0:1], -1.0,
                            ALU.mult, ALU.mult).inc("vq", 1, ("p5_nmr", i))
            v.wait(("p5_nmr", i))
            v.tensor_scalar(x1nb[:, b, :], x1rw[:, b, :], rho[:, b, 0:1],
                            nmr[:, b, 0:1], ALU.mult, ALU.add).inc("vq", 1, ("p5_xn", i))
        for hf in range(NHALF):
            if hf == 1:
                v.wait(("p6_sg", NPATH * 32 - 1))
            v.memset(alpha[:, :, NE:NE + 1], 1.0).inc("vq", 1, ("p6_a8", hf))
            for ts in range(TSH):
                v.wait(("p6_lg_ps", (hf, ts)))
                v.tensor_copy(lgts[:, ts, :],
                              PS[:, 3072 + (ts % 2) * 512:3072 + (ts % 2) * 512 + 8]
                              ).inc("vq", 1, ("p6_lg", (hf, ts)))
                v.wait(("p6_lg", (hf, ts)))
                v.tensor_reduce(lmax[:, ts, :], lgts[:, ts, :], AX.X, ALU.max).inc(
                    "vq", 1, ("p6_lmax", (hf, ts)))
                v.wait(("p6_sigm", (hf, ts)))
                v.tensor_scalar(sigm[:, ts, :], sigm[:, ts, :], 1.0, 1.0,
                                ALU.mult, ALU.add).inc("vq", 1, ("p6_s1", (hf, ts)))
                v.wait(("p6_s1", (hf, ts)))
                v.reciprocal(sigm[:, ts, :], sigm[:, ts, :]).inc("vq", 1, ("p6_s2", (hf, ts)))
                v.wait(("p6_s2", (hf, ts)))
                v.tensor_tensor(msk8[:, ts, :], lgts[:, ts, :],
                                lmax[:, ts, :].broadcast_to([128, NE]),
                                ALU.is_equal).inc("vq", 1, ("p6_m", (hf, ts)))
                v.wait(("p6_m", (hf, ts)))
                v.tensor_scalar(alpha[:, ts, 0:NE], msk8[:, ts, :], sigm[:, ts, 0:1],
                                None, ALU.mult).inc("vq", 1, ("p6_al", (hf, ts)))
            for e in range(NPATH):
                for f4 in range(4):
                    l = (hf * NPATH + e) * 4 + f4
                    for ts in range(TSH):
                        u = (hf * NPATH + e) * 32 + f4 * 8 + ts
                        b = u % 2
                        sid = l * 2 + ts // 4
                        v.wait(("p6_sg", u))
                        if ts % 4 == 0 and sid >= 2:
                            v.wait(("p6_hsp", sid - 2))
                        v.scalar_tensor_tensor(hst[:, sid % 2, ts % 4, :],
                                               PS[:, (b * 2 + 1) * 512:(b * 2 + 1) * 512 + 512],
                                               alpha[:, ts, e:e + 1], sgb[:, b, :],
                                               ALU.mult, ALU.mult).inc("vq", 1, ("p6_h", u))
                for dt in range(ND):
                    l = (hf * NPATH + e) * ND + dt
                    bb = dt % 2
                    v.wait(("p6_y_ps", l))
                    if hf == 1 and e == 0 and dt == 0:
                        v.wait(("p6_ysp", 0))
                    if e == 0:
                        v.tensor_copy(ycm[:, dt, :], PS[:, bb * 1024:bb * 1024 + 1024]).inc(
                            "vq", 1, ("p6_y", l))
                    else:
                        v.tensor_tensor(ycm[:, dt, :], ycm[:, dt, :],
                                        PS[:, bb * 1024:bb * 1024 + 1024], ALU.add).inc(
                            "vq", 1, ("p6_y", l))
            v.wait(("p6_y", (hf * NPATH + NPATH) * ND - 1))
            v.tensor_copy(smx[:, 0:1], smx[:, 0:1]).inc("vq", 1, ("p6_ylast", hf))
            for j in range(TSH):
                jj = hf * TSH + j
                v.wait(("p7_y", jj))
                v.wait(("p7_x1", jj))
                v.tensor_tensor(xtl[:, j % 2, :], xtl[:, j % 2, :], ytl[:, j % 2, :],
                                ALU.add).inc("vq", 1, ("p7_add", jj))

    def emit_tensor(t):
        t.wait("p0_ones")
        for bi in range(4):
            for c in range(4):
                t.wait(("p0_brow", (bi, c)))
                if bi > 0:
                    t.wait(("p0_bias", (bi - 1, c)))
                t.matmul(PS[:, c * 512:(c + 1) * 512], ones1, brow,
                         start=True, stop=True).inc("pq", 1, ("p0_bias_ps", (bi, c)))
        for q8 in range(8):
            for m in range(3):
                t.wait(("p2_slab", (q8, m)))
            for ts in range(NT):
                b = ts % 2
                idx = q8 * NT + ts
                if idx < 2:
                    t.wait(("p0_bias", (3, 3)))
                    for kb in range(ND):
                        t.wait(("p2_aff", kb))
                else:
                    for m in range(3):
                        t.wait(("p2_evac", ((idx - 2) // NT, (idx - 2) % NT, m)))
                last = None
                for kb in range(ND):
                    for m in range(3):
                        t.matmul(PS[:, (m * 2 + b) * 512:(m * 2 + b) * 512 + 256],
                                 xnT[:, kb, ts * 128:(ts + 1) * 128],
                                 wqkv[:, q8 % 2, m, kb, :],
                                 start=(kb == 0), stop=(kb == ND - 1))
                t.inc("pq", 1, ("p2_mm", (q8, ts)))
        for p in range(2):
            t.wait(("p4_slab", (p, 3)))
            for ts in range(NT):
                for jq in range(4):
                    idx = p * 64 + ts * 4 + jq
                    bb = idx % 4
                    if idx < 4:
                        t.wait(("p2_evac", (7, NT - 1, 2)))
                        for kb in range(ND):
                            t.wait(("p4_ctxT", kb))
                    else:
                        t.wait(("p4_evac", idx - 4))
                    for kb in range(ND):
                        t.matmul(PS[:, bb * 512:bb * 512 + 256],
                                 ctxT[:, kb, ts * 128:(ts + 1) * 128],
                                 wo_s[:, jq, kb, :],
                                 start=(kb == 0), stop=(kb == ND - 1))
                    t.inc("pq", 1, ("p4_mm", idx))
        for hf in range(NHALF):
            for ts in range(TSH):
                if ts == 0:
                    t.wait(("p6_x1nT", (hf, ND - 1)))
                    if hf == 0:
                        t.wait("p0_rw")
                if ts >= 2:
                    t.wait(("p6_lg", (hf, ts - 2)))
                for kb in range(ND):
                    t.matmul(PS[:, 3072 + (ts % 2) * 512:3072 + (ts % 2) * 512 + 8],
                             x1nT[:, kb, ts * 128:(ts + 1) * 128],
                             rw_s[:, kb, :],
                             start=(kb == 0), stop=(kb == ND - 1))
                t.inc("pq", 1, ("p6_lg_ps", (hf, ts)))
            for e in range(NPATH):
                for f4 in range(4):
                    l = (hf * NPATH + e) * 4 + f4
                    t.wait(("p6_gu", l))
                    for ts in range(TSH):
                        u = (hf * NPATH + e) * 32 + f4 * 8 + ts
                        b = u % 2
                        if u == 0 and hf == 0:
                            t.wait(("p4_evac", 8 * NT - 1))
                        if u >= 2:
                            t.wait(("p6_h", u - 2))
                        elif hf * NPATH + e >= 1:
                            t.wait(("p6_y", (hf * NPATH + e) * ND - 2 + (u % 2)))
                        for kb in range(ND):
                            t.matmul(PS[:, (b * 2) * 512:(b * 2) * 512 + 512],
                                     x1nT[:, kb, ts * 128:(ts + 1) * 128],
                                     gus[:, l % 2, 0, kb, :],
                                     start=(kb == 0), stop=(kb == ND - 1))
                            t.matmul(PS[:, (b * 2 + 1) * 512:(b * 2 + 1) * 512 + 512],
                                     x1nT[:, kb, ts * 128:(ts + 1) * 128],
                                     gus[:, l % 2, 1, kb, :],
                                     start=(kb == 0), stop=(kb == ND - 1))
                        t.inc("pq", 1, ("p6_gu_ps", u))
                for dt in range(ND):
                    l = (hf * NPATH + e) * ND + dt
                    bb = dt % 2
                    t.wait(("p6_dsl", l))
                    if dt == 0:
                        t.wait(("p6_h", (hf * NPATH + e) * 32 + 31))
                    if dt >= 2:
                        t.wait(("p6_y", l - 2))
                    for f in range(16):
                        if dt == 0:
                            t.wait(("p6_HT", (hf, e, f)))
                        t.matmul(PS[:, bb * 1024:bb * 1024 + 512],
                                 dsl[:, l % 4, f, :], HT[:, f, 0:512],
                                 start=(f == 0), stop=(f == 15))
                        t.matmul(PS[:, bb * 1024 + 512:bb * 1024 + 1024],
                                 dsl[:, l % 4, f, :], HT[:, f, 512:1024],
                                 start=(f == 0), stop=(f == 15))
                    t.inc("pq", 1, ("p6_y_ps", l))

    emitters = {"sync": emit_sync, "gpsimd": emit_gpsimd, "scalar": emit_scalar,
                "vector": emit_vector, "tensor": emit_tensor}
    for n, f in emitters.items():
        f(Eng(plan, sems))
    plan.recording = False

    with nc.Block() as block:
        block.sync(lambda e: emit_sync(Eng(plan, sems, e)))
        block.gpsimd(lambda e: emit_gpsimd(Eng(plan, sems, e)))
        block.scalar(lambda e: emit_scalar(Eng(plan, sems, e)))
        block.vector(lambda e: emit_vector(Eng(plan, sems, e)))
        block.tensor(lambda e: emit_tensor(Eng(plan, sems, e)))

    st.close()
    return nc


_NC_CACHE = {}


def kernel(**inputs):
    if "main" not in _NC_CACHE:
        _NC_CACHE["main"] = build_nc()
    nc = _NC_CACHE["main"]
    xs = np.ascontiguousarray(np.asarray(inputs["L0_attn_x"], dtype=np.float32))
    base = {
        "ln_g": np.ascontiguousarray(np.asarray(inputs["L0_attn_ln_g"], np.float32)),
        "ln_b": np.ascontiguousarray(np.asarray(inputs["L0_attn_ln_b"], np.float32)),
        "wq": np.ascontiguousarray(np.asarray(inputs["L0_attn_Wq"], np.float32)),
        "wk": np.ascontiguousarray(np.asarray(inputs["L0_attn_Wk"], np.float32)),
        "wv": np.ascontiguousarray(np.asarray(inputs["L0_attn_Wv"], np.float32)),
        "wo": np.ascontiguousarray(np.asarray(inputs["L0_attn_Wo"], np.float32)),
        "bq": np.ascontiguousarray(np.asarray(inputs["L0_attn_bq"], np.float32)),
        "bk": np.ascontiguousarray(np.asarray(inputs["L0_attn_bk"], np.float32)),
        "bv": np.ascontiguousarray(np.asarray(inputs["L0_attn_bv"], np.float32)),
        "bo": np.ascontiguousarray(np.asarray(inputs["L0_attn_bo"], np.float32)),
        "rw": np.ascontiguousarray(np.asarray(inputs["L0_ffn_router_w"], np.float32)),
        "gw": np.ascontiguousarray(np.asarray(inputs["L0_ffn_gate_w"], np.float32)),
        "uw": np.ascontiguousarray(np.asarray(inputs["L0_ffn_up_w"], np.float32)),
        "dw": np.ascontiguousarray(np.asarray(inputs["L0_ffn_down_w"], np.float32)),
        "sgw": np.ascontiguousarray(np.asarray(inputs["L0_ffn_shared_gate_w"], np.float32)),
        "suw": np.ascontiguousarray(np.asarray(inputs["L0_ffn_shared_up_w"], np.float32)),
        "sdw": np.ascontiguousarray(np.asarray(inputs["L0_ffn_shared_down_w"], np.float32)),
    }
    in_maps = [dict(base, x=xs[b]) for b in range(8)]
    res = run_bass_kernel_spmd(nc, in_maps, core_ids=list(range(8)))
    return np.stack([r["out"] for r in res.results], axis=0)


# version 06
# speedup vs baseline: 1.1824x; 1.1824x over previous
import numpy as np
from contextlib import ExitStack

import concourse.bass as bass
import concourse.mybir as mybir
from concourse.bass_utils import run_bass_kernel_spmd

F32 = mybir.dt.float32
BF16 = mybir.dt.bfloat16
AF = mybir.ActivationFunctionType
ALU = mybir.AluOpType
AX = mybir.AxisListType

T = 2048
D = 2048
H = 16
HD = 128
NE = 8
FF = 2048
NT = 16
ND = 16
EPS = 1e-5
INVSQD = float(1.0 / np.sqrt(2048.0))
NHALF = 2
TSH = 8
NPATH = NE + 1
NPTOT = NHALF * NPATH


class Plan:
    def __init__(self):
        self.cnt = {}
        self.ev = {}
        self.recording = True

    def inc(self, sem, n, tag):
        self.cnt[sem] = self.cnt.get(sem, 0) + n
        if tag is not None:
            assert tag not in self.ev, f"dup event {tag}"
            self.ev[tag] = (sem, self.cnt[sem])


class Eng:
    def __init__(self, plan, sems, real=None):
        self.plan = plan
        self.sems = sems
        self.real = real
        self._last = None

    def __getattr__(self, op):
        def call(*a, **kw):
            if self.plan.recording:
                self._last = None
                return self
            self._last = getattr(self.real, op)(*a, **kw)
            return self
        return call

    def inc(self, sem, n=1, tag=None):
        if self.plan.recording:
            self.plan.inc(sem, n, tag)
        else:
            self._last.then_inc(self.sems[sem], n)
        return self

    def wait(self, tag):
        if not self.plan.recording:
            sem, val = self.plan.ev[tag]
            self.real.wait_ge(self.sems[sem], val)
        return self


def build_nc(dbg=()):
    nc = bass.Bass("TRN2", target_bir_lowering=False, debug=False)
    st = ExitStack()

    def dram(name, shape, dtype):
        if name in dbg:
            if dtype == BF16:
                return nc.dram_tensor(name, shape, mybir.dt.uint16,
                                      kind="ExternalOutput").ap().bitcast(BF16)
            return nc.dram_tensor(name, shape, dtype, kind="ExternalOutput").ap()
        return nc.dram_tensor(name, shape, dtype, kind="Internal").ap()

    x_d = nc.dram_tensor("x", [T, D], F32, kind="ExternalInput").ap()
    lng_d = nc.dram_tensor("ln_g", [D], F32, kind="ExternalInput").ap()
    lnb_d = nc.dram_tensor("ln_b", [D], F32, kind="ExternalInput").ap()
    w_qkv_d = [nc.dram_tensor(n, [D, D], F32, kind="ExternalInput").ap()
               for n in ["wq", "wk", "wv"]]
    wo_d = nc.dram_tensor("wo", [D, D], F32, kind="ExternalInput").ap()
    b_qkvo_d = [nc.dram_tensor(n, [D], F32, kind="ExternalInput").ap()
                for n in ["bq", "bk", "bv", "bo"]]
    rw_d = nc.dram_tensor("rw", [D, NE], F32, kind="ExternalInput").ap()
    gw_d = nc.dram_tensor("gw", [NE, D, FF], F32, kind="ExternalInput").ap()
    uw_d = nc.dram_tensor("uw", [NE, D, FF], F32, kind="ExternalInput").ap()
    dw_d = nc.dram_tensor("dw", [NE, FF, D], F32, kind="ExternalInput").ap()
    sg_d = nc.dram_tensor("sgw", [D, FF], F32, kind="ExternalInput").ap()
    su_d = nc.dram_tensor("suw", [D, FF], F32, kind="ExternalInput").ap()
    sd_d = nc.dram_tensor("sdw", [FF, D], F32, kind="ExternalInput").ap()
    out_d = nc.dram_tensor("out", [T, D], F32, kind="ExternalOutput").ap()

    xn_d = dram("xn_s", [T, D], BF16)
    q_d = dram("q_s", [T, D], BF16)
    k_d = dram("k_s", [T, D], BF16)
    v_d = dram("v_s", [T, D], BF16)
    ctx_d = dram("ctx_s", [T, D], BF16)
    x1_d = dram("x1_s", [T, D], F32)
    x1n_d = dram("x1n_s", [T, D], BF16)
    h_d = dram("h_s", [T // 2, FF], BF16)
    y_d = dram("y_s", [D, T // 2], BF16)

    SB = lambda *a, **kw: st.enter_context(nc.sbuf_tensor(*a, **kw))[:]
    lng_cm = SB("lng_cm", [128, ND], F32)
    lnb_cm = SB("lnb_cm", [128, ND], F32)
    bo_b = SB("bo_b", [128, D], BF16)
    brow = SB("brow", [1, 512], F32)
    ones1 = SB("ones1", [1, 128], F32)
    rw_s = SB("rw_s", [128, ND, NE], BF16)
    alpha = SB("alpha", [128, TSH, NPATH], F32)
    stat4 = SB("stat4", [128, 2, 4, 6], F32)
    mv2 = SB("mv2", [128, 2, 2], F32)
    rho = SB("rho", [128, 2, 1], F32)
    nmr = SB("nmr", [128, 2, 1], F32)
    t128 = SB("t128", [128, 2, 1], F32)
    lgts = SB("lgts", [128, TSH, NE], F32)
    lmax = SB("lmax", [128, TSH, 1], F32)
    sigm = SB("sigm", [128, TSH, 1], F32)
    msk8 = SB("msk8", [128, TSH, NE], F32)
    smx = SB("smx", [128, 2, H], F32)

    MB = 1024 * 1024 // 128
    A8 = SB("A8", [128, 8 * MB], mybir.dt.uint8, align_bytes=512)
    B12 = SB("B12", [128, 12 * MB], mybir.dt.uint8, align_bytes=512)
    C2 = SB("C2", [128, 7 * MB // 2], mybir.dt.uint8, align_bytes=512)
    PS = st.enter_context(nc.psum_tensor("psall", [128, 4096], F32))[:]

    def carve(arena, off, shape, dtype):
        assert shape[0] == 128
        nby = int(np.prod(shape[1:])) * mybir.dt.size(dtype)
        v = arena[:, off:off + nby].bitcast(dtype)
        if len(shape) == 3:
            v = v.rearrange("p (a b) -> p a b", b=shape[2])
        elif len(shape) == 4:
            v = v.rearrange("p (a b c) -> p a b c", b=shape[2], c=shape[3])
        elif len(shape) == 5:
            v = v.rearrange("p (a b c d) -> p a b c d",
                            b=shape[2], c=shape[3], d=shape[4])
        return v, off + nby

    xnbuf, o = carve(A8, 0, [128, 2, D], BF16)
    xbuf, _ = carve(A8, o, [128, 2, D], F32)
    xnT, _ = carve(A8, 0, [128, ND, T], BF16)
    gus, _ = carve(A8, 0, [128, 2, 2, ND, 256], BF16)
    o = 4 * MB
    qt1, o = carve(A8, o, [128, D], BF16)
    kt1, o = carve(A8, o, [128, D], BF16)
    vt1, o = carve(A8, o, [128, D], BF16)
    prodb1, o = carve(A8, o, [128, H, HD], BF16)
    scor1, o = carve(A8, o, [128, H, H], F32)
    wsm1, o = carve(A8, o, [128, H, H], F32)
    ctxf1, o = carve(A8, o, [128, H, HD], BF16)
    ytl, o = carve(A8, o, [128, 2, D], BF16)
    assert o <= 8 * MB, o
    xtl, _ = carve(A8, 4 * MB, [128, 2, D], F32)

    wqkv, _ = carve(B12, 0, [128, 2, 3, ND, 256], BF16)
    qt0, o = carve(B12, 0, [128, 2, D], BF16)
    kt0, o = carve(B12, o, [128, 2, D], BF16)
    vt0, o = carve(B12, o, [128, 2, D], BF16)
    prodb0, o = carve(B12, o, [128, 2, H, HD], BF16)
    scor0, o = carve(B12, o, [128, 2, H, H], F32)
    wsm0, o = carve(B12, o, [128, 2, H, H], F32)
    ctxf0, o = carve(B12, o, [128, 2, H, HD], F32)
    ctxb0, o = carve(B12, o, [128, 2, D], BF16)
    bqkv_b, o = carve(B12, o, [128, 3, D], BF16)
    assert o <= 12 * MB, o
    ctxTh, o = carve(B12, 0, [128, ND, T // 2], BF16)
    wo_s, o = carve(B12, o, [128, 8, ND, 256], BF16)
    ycm, o = carve(B12, 0, [128, ND, T // 2], BF16)
    x1nT, o6 = carve(B12, o, [128, ND, T // 2], BF16)
    HT, _ = carve(B12, o6, [128, 16, T // 2], BF16)
    x1rw, _ = carve(C2, 2 * MB, [128, D], F32)

    wq2, _ = carve(C2, 0, [128, 2, 3, ND, 128], BF16)
    xrl, o = carve(C2, 0, [128, 4, 256], F32)
    x1st, o = carve(C2, o, [128, 4, 256], F32)
    x1nb, o = carve(C2, o, [128, 2, D], BF16)
    hst, o = carve(C2, 0, [128, 2, 4, 256], BF16)
    sgb, o = carve(C2, o, [128, 2, 256], F32)
    dsl, o = carve(C2, o, [128, 4, 16, 128], BF16)
    qkve, o = carve(C2, 3 * MB, [128, 2, 3, 256], BF16)
    assert o <= 7 * MB // 2, o

    plan = Plan()

    class SemDict(dict):
        def __missing__(self, n):
            h = st.enter_context(nc.semaphore(n))
            self[n] = h
            return h
    sems = SemDict()

    H0_LAST_U = NPATH * 64 - 1
    H0_LAST_Y = NPATH * ND - 1
    H0_LAST_SID = NPATH * 8 * 2 - 1

    def cbufs(i):
        if i < 8:
            b = i % 2
            return dict(qt=qt0[:, b, :], kt=kt0[:, b, :], vt=vt0[:, b, :],
                        prod=lambda pb: prodb0[:, pb, :, :],
                        scor=scor0[:, b, :, :], wsm=wsm0[:, b, :, :],
                        ctxrow=lambda hh: ctxf0[:, b, hh, :],
                        ctxout=lambda hh: ctxb0[:, b, hh * 128:(hh + 1) * 128],
                        spill=ctxb0[:, b, :], sm=smx[:, 0, :])
        return dict(qt=qt1, kt=kt1, vt=vt1,
                    prod=lambda pb: prodb1,
                    scor=scor1, wsm=wsm1,
                    ctxrow=lambda hh: ctxf1[:, hh, :],
                    ctxout=lambda hh: ctxf1[:, hh, :],
                    spill=ctxf1.rearrange("p h d -> p (h d)"), sm=smx[:, 1, :])

    def c_vec_scores_ops(i):
        cb = cbufs(i)
        ops = []

        def mkprod(h):
            def f(v):
                pb = h % 2
                if h == 0:
                    v.wait(("p3_qkv", i))
                    if i < 8 and i >= 2:
                        v.wait(("p3_exp", i - 2))
                    if i > 8:
                        v.wait(("p3_exp", i - 1))
                if h >= 2:
                    v.wait(("p3_red", (i, h - 2)))
                qv = cb["qt"][:, h * 128:(h + 1) * 128].unsqueeze(1).broadcast_to([128, H, HD])
                kv = cb["kt"].rearrange("p (g d) -> p g d", d=HD)
                v.tensor_tensor(cb["prod"](pb), qv, kv, ALU.mult).inc("vq", 1, ("p3_prod", (i, h)))
            return f

        def mkred(h):
            def f(v):
                pb = h % 2
                v.wait(("p3_prod", (i, h)))
                v.tensor_reduce(cb["scor"][:, h, :], cb["prod"](pb), AX.X, ALU.add).inc(
                    "vq", 1, ("p3_red", (i, h)))
            return f
        for h in range(H):
            ops.append(mkprod(h))
            ops.append(mkred(h))

        def fmx(v):
            v.wait(("p3_red", (i, H - 1)))
            v.tensor_reduce(cb["sm"], cb["scor"], AX.X, ALU.max).inc("vq", 1, ("p3_mx", i))
        ops.append(fmx)

        def fsmx(v):
            v.wait(("p3_mx", i))
            v.tensor_tensor(cb["scor"], cb["scor"],
                            cb["sm"].unsqueeze(2).broadcast_to([128, H, H]),
                            ALU.subtract).inc("vq", 1, ("p3_smx", i))
        ops.append(fsmx)
        return ops

    def c_vec_ctx_ops(i):
        cb = cbufs(i)
        ops = []

        def fsum(v):
            v.wait(("p3_exp", i))
            v.tensor_reduce(cb["sm"], cb["wsm"], AX.X, ALU.add).inc("vq", 1, ("p3_sum", i))
        ops.append(fsum)

        def frec(v):
            v.wait(("p3_sum", i))
            v.reciprocal(cb["sm"], cb["sm"]).inc("vq", 1, ("p3_rec", i))
        ops.append(frec)

        def fw(v):
            v.wait(("p3_rec", i))
            v.tensor_tensor(cb["wsm"], cb["wsm"],
                            cb["sm"].unsqueeze(2).broadcast_to([128, H, H]),
                            ALU.mult).inc("vq", 1, ("p3_w", i))
        ops.append(fw)

        def mkctx(gg, h):
            def f(v):
                if gg == 0 and h == 0:
                    v.wait(("p3_w", i))
                    if i > 8:
                        v.wait(("p3_ctxsp", i - 1))
                if i < 8 and gg == 15 and h == 0 and i >= 2:
                    v.wait(("p3_ctxsp", i - 2))
                dst = cb["ctxout"](h) if gg == H - 1 else cb["ctxrow"](h)
                if gg == 0:
                    v.tensor_scalar(dst, cb["vt"][:, 0:128],
                                    cb["wsm"][:, h, 0:1], None, ALU.mult).inc(
                        "vq", 1, ("p3_c", (i, gg, h)))
                else:
                    v.wait(("p3_c", (i, gg - 1, h)))
                    v.scalar_tensor_tensor(dst, cb["vt"][:, gg * 128:(gg + 1) * 128],
                                           cb["wsm"][:, h, gg:gg + 1], cb["ctxrow"](h),
                                           ALU.mult, ALU.add).inc("vq", 1, ("p3_c", (i, gg, h)))
            return f
        for gg in range(H):
            for h in range(H):
                ops.append(mkctx(gg, h))
        return ops

    def c_act_exp(a, i):
        cb = cbufs(i)
        a.wait(("p3_smx", i))
        if i > 8:
            a.wait(("p3_c", (i - 1, H - 1, H - 1)))
        a.activation(cb["wsm"], cb["scor"], AF.Exp, scale=INVSQD).inc("aq", 1, ("p3_exp", i))

    def c_act_spill(a, i):
        cb = cbufs(i)
        a.wait(("p3_c", (i, H - 1, H - 1)))
        a.dma_start(ctx_d[i * 128:(i + 1) * 128, :], cb["spill"]).inc(
            f"a_ctxsp{i % 2}", 16, ("p3_ctxsp", i))

    def emit_sync(s):
        for bi in range(4):
            for c in range(4):
                if (bi, c) > (0, 0):
                    pc = bi * 4 + c - 1
                    s.wait(("p0_bias_ps", (pc // 4, pc % 4)))
                s.dma_start(brow, b_qkvo_d[bi][c * 512:(c + 1) * 512].unsqueeze(0)
                            ).inc("s_brow", 16, ("p0_brow", (bi, c)))
        for i in range(NT):
            if i >= 2:
                s.wait(("p1_xn", i - 2))
            s.dma_start(xbuf[:, i % 2, :], x_d[i * 128:(i + 1) * 128, :]).inc(
                f"s_x{i % 2}", 16, ("p1_x", i))
        for i in range(8):
            if i < 2:
                s.wait(("p2_mm", 63))
            else:
                s.wait(("p3_c", (i - 2, H - 1, H - 1)))
            s.wait(("p2_store", (7, i, 2)))
            s.dma_start(qt0[:, i % 2, :], q_d[i * 128:(i + 1) * 128, :]).inc(f"s_qkv{i % 2}", 16)
            s.dma_start(kt0[:, i % 2, :], k_d[i * 128:(i + 1) * 128, :]).inc(f"s_qkv{i % 2}", 16)
            s.dma_start(vt0[:, i % 2, :], v_d[i * 128:(i + 1) * 128, :]).inc(
                f"s_qkv{i % 2}", 16, ("p3_qkv", i))
            if i == 1:
                for bidx2 in range(128):
                    q16, rr = divmod(bidx2, 8)
                    ts = 8 + rr
                    for m, dd in [(0, q_d), (1, k_d), (2, v_d)]:
                        s.wait(("p2_evac2", (q16, ts, m)))
                        s.dma_start(dd[ts * 128:(ts + 1) * 128, q16 * 128:(q16 + 1) * 128],
                                    qkve[:, bidx2 % 2, m, 0:128]
                                    ).inc(f"a_q2sp{bidx2 % 2}", 16, ("p2_store2", (q16, ts, m)))
        for idx in range(64):
            g, r = divmod(idx, 16)
            tsl, jq2 = divmod(r, 2)
            ts, q8 = tsl, g * 2 + jq2
            if idx < 4:
                s.wait(("p1_xn", NT - 1))
                s.wait(("p2_mm2", 127))
            else:
                s.wait(("p4_evac", idx - 4))
            s.dma_start(xrl[:, idx % 4, :],
                        x_d[ts * 128:(ts + 1) * 128, q8 * 256:(q8 + 1) * 256]
                        ).inc(f"s_xrl{idx % 4}", 16, ("p4_x", idx))
            if g == 3 and jq2 == 1:
                if tsl >= 1:
                    s.wait(("p5_xn", tsl - 1))
                s.wait(("p4_x1sp", idx))
                s.dma_start(x1rw, x1_d[ts * 128:(ts + 1) * 128, :]).inc(
                    "s_x1rw0", 16, ("p5_x1", ts))
        for i in range(8, 16):
            if i == 8:
                s.wait(("p4_mm", 63))
            else:
                s.wait(("p3_c", (i - 1, H - 1, H - 1)))
            s.wait(("p2_store2", (15, i, 2)))
            s.wait(("p2_store2", (14, i, 2)))
            s.dma_start(qt1, q_d[i * 128:(i + 1) * 128, :]).inc("s_qkv1h", 16)
            s.dma_start(kt1, k_d[i * 128:(i + 1) * 128, :]).inc("s_qkv1h", 16)
            s.dma_start(vt1, v_d[i * 128:(i + 1) * 128, :]).inc("s_qkv1h", 16, ("p3_qkv", i))
        for idx in range(64, 128):
            g, r = divmod(idx - 64, 16)
            tsl, jq2 = divmod(r, 2)
            ts, q8 = 8 + tsl, g * 2 + jq2
            if idx < 68:
                s.wait(("p6_hsp", H0_LAST_SID))
                s.wait(("p6_h", H0_LAST_U))
                s.wait(("p6_y_ps", H0_LAST_Y))
            s.wait(("p4_evac", idx - 4))
            s.dma_start(xrl[:, idx % 4, :],
                        x_d[ts * 128:(ts + 1) * 128, q8 * 256:(q8 + 1) * 256]
                        ).inc(f"s_xrl{idx % 4}", 16, ("p4_x", idx))
            if g == 3 and jq2 == 1:
                if tsl >= 1:
                    s.wait(("p5_xn", ts - 1))
                else:
                    s.wait(("p5_xn", 7))
                    s.wait(("p6_y_ps", H0_LAST_Y))
                s.wait(("p4_x1sp", idx))
                s.dma_start(x1rw, x1_d[ts * 128:(ts + 1) * 128, :]).inc(
                    "s_x1rw0", 16, ("p5_x1", ts))
        for hf in range(NHALF):
            for j in range(TSH):
                jj = hf * TSH + j
                r = hf * 1024 + j * 128
                if j == 0:
                    s.wait(("p6_y_ps", (hf * NPATH + NPATH) * ND - 1))
                    if hf == 0:
                        s.wait(("p3_ctxsp", 15))
                if jj >= 2:
                    s.wait(("p7_out", jj - 2))
                s.dma_start(xtl[:, j % 2, :], x1_d[r:r + 128, :]).inc(
                    f"s_xtl{j % 2}", 16, ("p7_x1", jj))
                s.wait(("p7_add", jj))
                s.dma_start(out_d[r:r + 128, :], xtl[:, j % 2, :]).inc(
                    f"s_out{j % 2}", 16, ("p7_out", jj))
        s.wait(("p7_out", NHALF * TSH - 2))
        s.wait(("p7_out", NHALF * TSH - 1))

    def emit_gpsimd(g):
        g.dma_start(lng_cm, lng_d.rearrange("(k p) -> p k", p=128),
                    allow_slow_non_contiguous=True).inc("g_small", 16)
        g.dma_start(lnb_cm, lnb_d.rearrange("(k p) -> p k", p=128),
                    allow_slow_non_contiguous=True).inc("g_small", 16, "p0_ln")
        g.dma_start(rw_s, rw_d.rearrange("(k p) e -> p k e", p=128),
                    allow_slow_non_contiguous=True).inc("g_small", 16, "p0_rw")
        for q8 in range(8):
            for m in range(3):
                if q8 >= 2 and m == 0:
                    g.wait(("p2_mm", (q8 - 2) * 8 + 7))
                g.dma_start(wqkv[:, q8 % 2, m, :, :],
                            w_qkv_d[m].rearrange("(k p) n -> p k n", p=128)
                            [:, :, q8 * 256:(q8 + 1) * 256]
                            ).inc(f"g_wqkv{q8 % 2}", 16, ("p2_slab", (q8, m)))
        for q16 in range(16):
            for m in range(3):
                if q16 >= 2 and m == 0:
                    g.wait(("p2_mm2", (q16 - 2) * 8 + 7))
                g.dma_start(wq2[:, q16 % 2, m, :, :],
                            w_qkv_d[m].rearrange("(k p) n -> p k n", p=128)
                            [:, :, q16 * 128:(q16 + 1) * 128]
                            ).inc(f"g_wq2{q16 % 2}", 16, ("p2_slab2", (q16, m)))
        for q8 in range(8):
            if q8 == 0:
                g.wait(("p3_ctxsp", 7))
            g.dma_start(wo_s[:, q8, :, :],
                        wo_d.rearrange("(k p) n -> p k n", p=128)
                        [:, :, q8 * 256:(q8 + 1) * 256]).inc(
                f"g_wo{q8 % 4}", 16, ("p4_slab", (0, q8)))

        def gus_load(path, f8):
            hf, e = divmod(path, NPATH)
            ge = gw_d[e] if e < NE else sg_d
            ue = uw_d[e] if e < NE else su_d
            l8 = path * 8 + f8
            if l8 < 2:
                g.wait(("p2_mm2", 127))
            else:
                g.wait(("p6_gu_ps", (l8 - 2) * 8 + 7))
            g.dma_start(gus[:, l8 % 2, 0, :, :],
                        ge.rearrange("(k p) f -> p k f", p=128)
                        [:, :, f8 * 256:(f8 + 1) * 256]).inc(f"g_gus{l8 % 2}", 16)
            g.dma_start(gus[:, l8 % 2, 1, :, :],
                        ue.rearrange("(k p) f -> p k f", p=128)
                        [:, :, f8 * 256:(f8 + 1) * 256]).inc(
                f"g_gus{l8 % 2}", 16, ("p6_gu", l8))

        def dsl_load(path, dt):
            hf, e = divmod(path, NPATH)
            de = dw_d[e] if e < NE else sd_d
            l = path * ND + dt
            if l < 4:
                g.wait(("p5_sp", TSH - 1))
                g.wait(("p5_sp", TSH - 2))
            elif path == NPATH and dt < 4:
                g.wait(("p5_sp", NT - 1))
                g.wait(("p5_sp", NT - 2))
                g.wait(("p6_y_ps", l - 4))
            else:
                g.wait(("p6_y_ps", l - 4))
            g.dma_start(dsl[:, l % 4, :, :],
                        de.rearrange("(k p) d -> p k d", p=128)
                        [:, :, dt * 128:(dt + 1) * 128]).inc(f"g_dsl{l % 4}", 16, ("p6_dsl", l))

        for f8 in range(8):
            gus_load(0, f8)
        for path in range(NPTOT):
            for dt in range(4):
                dsl_load(path, dt)
            if path + 1 < NPTOT:
                gus_load(path + 1, 0)
                gus_load(path + 1, 1)
            for dt in range(4, ND):
                dsl_load(path, dt)
            if path == NPATH - 1:
                for q8 in range(8):
                    if q8 == 0:
                        g.wait(("p6_gu_ps", H0_LAST_U))
                    if q8 == 4:
                        g.wait(("p6_y_ps", H0_LAST_Y))
                    g.dma_start(wo_s[:, q8, :, :],
                                wo_d.rearrange("(k p) n -> p k n", p=128)
                                [:, :, q8 * 256:(q8 + 1) * 256]).inc(
                        f"g_wo{q8 % 4}", 16, ("p4_slab", (1, q8)))
            if path + 1 < NPTOT:
                for f8 in range(2, 8):
                    gus_load(path + 1, f8)

    def emit_scalar(a):
        for i in range(NT):
            b = i % 2
            a.wait(("p1_t128", i))
            a.activation(rho[:, b, :], t128[:, b, :], AF.Ln).inc("aq", 1, ("p1_lnx", i))
            a.wait(("p1_lnx", i))
            a.activation(rho[:, b, :], rho[:, b, :], AF.Exp, scale=-0.5).inc("aq", 1, ("p1_rho", i))
            a.wait(("p1_xn", i))
            a.dma_start(xn_d[i * 128:(i + 1) * 128, :], xnbuf[:, b, :]).inc(
                f"a_xnsp{b}", 16, ("p1_sp", i))
        a.wait(("p1_sp", NT - 1))
        a.wait(("p1_sp", NT - 2))
        for kb in range(ND):
            a.dma_start_transpose(xnT[:, kb, :],
                                  xn_d[:, kb * 128:(kb + 1) * 128]).inc(
                "a_xnT", 16, ("p2_xnT", kb))
        for q8 in range(8):
            for ts in range(8):
                for m, dd in [(0, q_d), (1, k_d), (2, v_d)]:
                    a.wait(("p2_evac", (q8, ts, m)))
                    a.dma_start(dd[ts * 128:(ts + 1) * 128, q8 * 256:(q8 + 1) * 256],
                                qkve[:, ts % 2, m, :]).inc(
                        f"a_qkvsp{ts % 2}", 16, ("p2_store", (q8, ts, m)))
        for i in range(8):
            if i >= 2:
                a.wait(("p3_c", (i - 2, H - 1, H - 1)))
            c_act_exp(a, i)
            if i >= 1:
                c_act_spill(a, i - 1)
        c_act_spill(a, 7)
        for kb in range(ND):
            a.dma_start_transpose(ctxTh[:, kb, :],
                                  ctx_d[0:1024, kb * 128:(kb + 1) * 128]).inc(
                "a_ctxT", 16, ("p4_ctxT", (0, kb)))
        for idx in range(64):
            g, r = divmod(idx, 16)
            tsl, jq2 = divmod(r, 2)
            ts, q8 = tsl, g * 2 + jq2
            a.wait(("p4_evac", idx))
            a.dma_start(x1_d[ts * 128:(ts + 1) * 128, q8 * 256:(q8 + 1) * 256],
                        x1st[:, idx % 4, :]).inc(f"a_x1sp{idx % 4}", 16, ("p4_x1sp", idx))
            if g == 3 and jq2 == 1:
                i = ts
                b = i % 2
                a.wait(("p5_t128", i))
                a.activation(rho[:, b, :], t128[:, b, :], AF.Ln).inc("aq", 1, ("p5_lnx", i))
                a.wait(("p5_lnx", i))
                a.activation(rho[:, b, :], rho[:, b, :], AF.Exp, scale=-0.5).inc(
                    "aq", 1, ("p5_rho", i))
                a.wait(("p5_xn", i))
                a.dma_start(x1n_d[i * 128:(i + 1) * 128, :], x1nb[:, b, :]).inc(
                    f"a_x1nsp{b}", 16, ("p5_sp", i))
        for kb in range(ND):
            if kb == 0:
                a.wait(("p5_sp", TSH - 1))
                a.wait(("p5_sp", TSH - 2))
            a.dma_start_transpose(x1nT[:, kb, :],
                                  x1n_d[0:1024, kb * 128:(kb + 1) * 128]).inc(
                "a_x1nT", 16, ("p6_x1nT", (0, kb)))
        for ts in range(TSH):
            a.wait(("p6_lmax", (0, ts)))
            a.activation(sigm[:, ts, :], lmax[:, ts, :], AF.Exp, scale=-1.0).inc(
                "aq", 1, ("p6_sigm", (0, ts)))
        for hf in range(NHALF):
            base = hf * NPATH
            a.wait(("p6_al", (hf, TSH - 1)))
            a.wait(("p6_a8", hf))
            for e in range(NPATH):
                path = base + e
                for f8 in range(8):
                    l8 = path * 8 + f8
                    for ts in range(TSH):
                        u = l8 * 8 + ts
                        b4 = u % 4
                        a.wait(("p6_gu_ps", u))
                        if u >= 2:
                            a.wait(("p6_h", u - 2))
                        a.activation(sgb[:, u % 2, :], PS[:, b4 * 512:b4 * 512 + 256],
                                     AF.Silu, scale=alpha[:, ts, e:e + 1]).inc(
                            "aq", 1, ("p6_sg", u))
                    for jh in range(2):
                        sid = l8 * 2 + jh
                        a.wait(("p6_h", l8 * 8 + jh * 4 + 3))
                        a.dma_start(h_d.rearrange("(s p) f -> p s f", p=128)
                                    [:, jh * 4:jh * 4 + 4, f8 * 256:(f8 + 1) * 256],
                                    hst[:, sid % 2, :, :]).inc(
                            f"a_hsp{sid % 2}", 16, ("p6_hsp", sid))
                    for fbb in (f8 * 2, f8 * 2 + 1):
                        if fbb == 0:
                            if path >= 1:
                                a.wait(("p6_y_ps", path * ND - 1))
                            if hf == 1:
                                a.wait(("p7_add", TSH - 1))
                                a.wait(("p4_mm", 127))
                        a.wait(("p6_hsp", l8 * 2))
                        a.wait(("p6_hsp", l8 * 2 + 1))
                        a.dma_start_transpose(HT[:, fbb, :],
                                              h_d[:, fbb * 128:(fbb + 1) * 128]).inc(
                            f"a_HT{fbb % 8}", 16, ("p6_HT", (hf, e, fbb)))
                if hf == 0:
                    if e < 8:
                        c_act_exp(a, 8 + e)
                    if 1 <= e <= 8:
                        c_act_spill(a, 8 + e - 1)
            a.wait(("p6_ylast", hf))
            a.dma_start(y_d.rearrange("(k p) t -> p k t", p=128), ycm).inc(
                "a_ysp", 16, ("p6_ysp", hf))
            if hf == 0:
                for kb in range(ND):
                    if kb == 0:
                        a.wait(("p6_ysp", 0))
                    a.dma_start_transpose(ctxTh[:, kb, :],
                                          ctx_d[1024:2048, kb * 128:(kb + 1) * 128]).inc(
                        "a_ctxT", 16, ("p4_ctxT", (1, kb)))
            def ytl_load(j, hf=hf):
                jj = hf * TSH + j
                if j == 0:
                    a.wait(("p6_y_ps", (hf * NPATH + NPATH) * ND - 1))
                    if hf == 0:
                        a.wait(("p3_ctxsp", 15))
                a.wait(("p6_ysp", hf))
                if jj >= 2:
                    a.wait(("p7_add", jj - 2))
                a.dma_start_transpose(ytl[:, j % 2, :],
                                      y_d[:, j * 128:(j + 1) * 128]).inc(
                    f"a_ytl{j % 2}", 16, ("p7_y", jj))
            for j in range(2):
                ytl_load(j)
            if hf == 1:
                for j in range(2, TSH):
                    ytl_load(j)
            if hf == 0:
                for idx in range(64, 128):
                    g, r = divmod(idx - 64, 16)
                    tsl, jq2 = divmod(r, 2)
                    ts, q8 = 8 + tsl, g * 2 + jq2
                    a.wait(("p4_evac", idx))
                    a.dma_start(x1_d[ts * 128:(ts + 1) * 128, q8 * 256:(q8 + 1) * 256],
                                x1st[:, idx % 4, :]).inc(
                        f"a_x1sp{idx % 4}", 16, ("p4_x1sp", idx))
                    if g == 3 and jq2 == 1:
                        i = ts
                        b = i % 2
                        a.wait(("p5_t128", i))
                        a.activation(rho[:, b, :], t128[:, b, :], AF.Ln).inc(
                            "aq", 1, ("p5_lnx", i))
                        a.wait(("p5_lnx", i))
                        a.activation(rho[:, b, :], rho[:, b, :], AF.Exp, scale=-0.5).inc(
                            "aq", 1, ("p5_rho", i))
                        a.wait(("p5_xn", i))
                        a.dma_start(x1n_d[i * 128:(i + 1) * 128, :], x1nb[:, b, :]).inc(
                            f"a_x1nsp{b}", 16, ("p5_sp", i))
                for kb in range(ND):
                    if kb == 0:
                        a.wait(("p5_sp", NT - 1))
                        a.wait(("p5_sp", NT - 2))
                        a.wait(("p4_mm", 127))
                    a.dma_start_transpose(x1nT[:, kb, :],
                                          x1n_d[1024:2048, kb * 128:(kb + 1) * 128]).inc(
                        "a_x1nT", 16, ("p6_x1nT", (1, kb)))
                for ts in range(TSH):
                    a.wait(("p6_lmax", (1, ts)))
                    a.activation(sigm[:, ts, :], lmax[:, ts, :], AF.Exp, scale=-1.0).inc(
                        "aq", 1, ("p6_sigm", (1, ts)))
                for j in range(2, TSH):
                    ytl_load(j, hf=0)

    def emit_vector(v):
        v.memset(ones1, 1.0).inc("vq", 1, "p0_ones")
        for bi in range(4):
            for c in range(4):
                v.wait(("p0_bias_ps", (bi, c)))
                dstb = bqkv_b[:, bi, :] if bi < 3 else bo_b
                v.tensor_copy(dstb[:, c * 512:(c + 1) * 512],
                              PS[:, c * 512:(c + 1) * 512]).inc("vq", 1, ("p0_bias", (bi, c)))
        for i in range(NT):
            b = i % 2
            v.wait(("p1_x", i))
            if i >= 2:
                v.wait(("p1_sp", i - 2))
            for c in range(4):
                v.bn_stats(stat4[:, b, c, :], xbuf[:, b, c * 512:(c + 1) * 512]).inc(
                    "vq", 1, ("p1_bn", (i, c)))
            v.wait(("p1_bn", (i, 3)))
            v.bn_aggr(mv2[:, b, :], stat4[:, b, :, :]).inc("vq", 1, ("p1_mv", i))
            v.wait(("p1_mv", i))
            v.tensor_scalar(t128[:, b, :], mv2[:, b, 1:2], 1.0 + EPS, EPS * EPS,
                            ALU.mult, ALU.add).inc("vq", 1, ("p1_t128", i))
            v.wait(("p1_rho", i))
            v.tensor_scalar(nmr[:, b, :], mv2[:, b, 0:1], rho[:, b, 0:1], -1.0,
                            ALU.mult, ALU.mult).inc("vq", 1, ("p1_nmr", i))
            v.wait(("p1_nmr", i))
            v.tensor_scalar(xnbuf[:, b, :], xbuf[:, b, :], rho[:, b, 0:1],
                            nmr[:, b, 0:1], ALU.mult, ALU.add).inc("vq", 1, ("p1_xn", i))
        for kb in range(ND):
            v.wait(("p2_xnT", ND - 1))
            if kb == 0:
                v.wait("p0_ln")
            v.tensor_scalar(xnT[:, kb, :], xnT[:, kb, :], lng_cm[:, kb:kb + 1],
                            lnb_cm[:, kb:kb + 1], ALU.mult, ALU.add).inc("vq", 1, ("p2_aff", kb))
        for q8 in range(8):
            for ts in range(8):
                b = ts % 2
                bidx = q8 * 8 + ts
                for m in range(3):
                    v.wait(("p2_mm", bidx))
                    if bidx >= 2:
                        v.wait(("p2_store", ((bidx - 2) // 8, (bidx - 2) % 8, 2)))
                    v.tensor_tensor(qkve[:, b, m, :],
                                    PS[:, (m * 2 + b) * 512:(m * 2 + b) * 512 + 256],
                                    bqkv_b[:, m, q8 * 256:(q8 + 1) * 256],
                                    ALU.add).inc("vq", 1, ("p2_evac", (q8, ts, m)))
        bh1 = []

        def mk_bh1(q16, ts):
            def f(vv):
                bidx2 = q16 * 8 + (ts - 8)
                b = ts % 2
                for m in range(3):
                    vv.wait(("p2_mm2", bidx2))
                    if bidx2 >= 2:
                        pq16, pts = (bidx2 - 2) // 8, (bidx2 - 2) % 8 + 8
                        vv.wait(("p2_store2", (pq16, pts, 2)))
                    vv.tensor_tensor(qkve[:, bidx2 % 2, m, 0:128],
                                     PS[:, (m * 2 + b) * 512:(m * 2 + b) * 512 + 128],
                                     bqkv_b[:, m, q16 * 128:(q16 + 1) * 128],
                                     ALU.add).inc("vq", 1, ("p2_evac2", (q16, ts, m)))
            return f
        for q16 in range(16):
            for ts in range(8, 16):
                bh1.append(mk_bh1(q16, ts))
        pump_state = {"i": 0}

        def pump(n):
            for _ in range(n):
                if pump_state["i"] < len(bh1):
                    bh1[pump_state["i"]](v)
                    pump_state["i"] += 1
        copnum = {"n": 0}
        for i in range(8):
            for op in c_vec_scores_ops(i) + c_vec_ctx_ops(i):
                op(v)
                copnum["n"] += 1
                if copnum["n"] % 2 == 0:
                    pump(1)
        while pump_state["i"] < len(bh1):
            pump(1)
        for hf in range(NHALF):
            base = hf * NPATH
            for idx in range(hf * 64, hf * 64 + 64):
                g, r = divmod(idx - hf * 64, 16)
                tsl, jq2 = divmod(r, 2)
                ts, q8 = hf * 8 + tsl, g * 2 + jq2
                bb = idx % 4
                v.wait(("p4_mm", idx))
                v.wait(("p4_x", idx))
                if idx >= 4:
                    v.wait(("p4_x1sp", idx - 4))
                v.tensor_tensor(x1st[:, bb, :], PS[:, bb * 512:bb * 512 + 256],
                                bo_b[:, q8 * 256:(q8 + 1) * 256], ALU.add).inc(
                    "vq", 1, ("p4_bo", idx))
                v.wait(("p4_bo", idx))
                v.tensor_tensor(x1st[:, bb, :], x1st[:, bb, :], xrl[:, bb, :],
                                ALU.add).inc("vq", 1, ("p4_evac", idx))
                if g == 3 and jq2 == 1:
                    i = ts
                    b = i % 2
                    v.wait(("p5_x1", i))
                    if i >= 2:
                        v.wait(("p5_sp", i - 2))
                    for c in range(4):
                        v.bn_stats(stat4[:, b, c, :], x1rw[:, c * 512:(c + 1) * 512]).inc(
                            "vq", 1, ("p5_bn", (i, c)))
                    v.wait(("p5_bn", (i, 3)))
                    v.bn_aggr(mv2[:, b, :], stat4[:, b, :, :]).inc("vq", 1, ("p5_mv", i))
                    v.wait(("p5_mv", i))
                    v.tensor_scalar(t128[:, b, :], mv2[:, b, 1:2], 1.0, EPS,
                                    ALU.mult, ALU.add).inc("vq", 1, ("p5_t128", i))
                    v.wait(("p5_rho", i))
                    v.tensor_scalar(nmr[:, b, :], mv2[:, b, 0:1], rho[:, b, 0:1], -1.0,
                                    ALU.mult, ALU.mult).inc("vq", 1, ("p5_nmr", i))
                    v.wait(("p5_nmr", i))
                    v.tensor_scalar(x1nb[:, b, :], x1rw, rho[:, b, 0:1],
                                    nmr[:, b, 0:1], ALU.mult, ALU.add).inc("vq", 1, ("p5_xn", i))
            if hf == 1:
                v.wait(("p6_sg", NPATH * 64 - 1))
            v.memset(alpha[:, :, NE:NE + 1], 1.0).inc("vq", 1, ("p6_a8", hf))
            for ts in range(TSH):
                v.wait(("p6_lg_ps", (hf, ts)))
                v.tensor_copy(lgts[:, ts, :],
                              PS[:, 3072 + (ts % 2) * 512:3072 + (ts % 2) * 512 + 8]
                              ).inc("vq", 1, ("p6_lg", (hf, ts)))
                v.wait(("p6_lg", (hf, ts)))
                v.tensor_reduce(lmax[:, ts, :], lgts[:, ts, :], AX.X, ALU.max).inc(
                    "vq", 1, ("p6_lmax", (hf, ts)))
                v.wait(("p6_sigm", (hf, ts)))
                v.tensor_scalar(sigm[:, ts, :], sigm[:, ts, :], 1.0, 1.0,
                                ALU.mult, ALU.add).inc("vq", 1, ("p6_s1", (hf, ts)))
                v.wait(("p6_s1", (hf, ts)))
                v.reciprocal(sigm[:, ts, :], sigm[:, ts, :]).inc("vq", 1, ("p6_s2", (hf, ts)))
                v.wait(("p6_s2", (hf, ts)))
                v.tensor_tensor(msk8[:, ts, :], lgts[:, ts, :],
                                lmax[:, ts, :].broadcast_to([128, NE]),
                                ALU.is_equal).inc("vq", 1, ("p6_m", (hf, ts)))
                v.wait(("p6_m", (hf, ts)))
                v.tensor_scalar(alpha[:, ts, 0:NE], msk8[:, ts, :], sigm[:, ts, 0:1],
                                None, ALU.mult).inc("vq", 1, ("p6_al", (hf, ts)))
            if hf == 1:
                for j in range(TSH):
                    v.wait(("p7_y", j))
                    v.wait(("p7_x1", j))
                    v.tensor_tensor(xtl[:, j % 2, :], xtl[:, j % 2, :], ytl[:, j % 2, :],
                                    ALU.add).inc("vq", 1, ("p7_add", j))
            for e in range(NPATH):
                path = base + e
                cl = []
                if hf == 0:
                    if 1 <= e <= 8:
                        cl += c_vec_ctx_ops(8 + e - 1)
                    if e < 8:
                        cl += c_vec_scores_ops(8 + e)
                ci = {"k": 0}

                def cpump(n, cl=cl, ci=ci):
                    for _ in range(n):
                        if ci["k"] < len(cl):
                            cl[ci["k"]](v)
                            ci["k"] += 1
                for f8 in range(8):
                    l8 = path * 8 + f8
                    for ts in range(TSH):
                        u = l8 * 8 + ts
                        b4 = u % 4
                        sid = l8 * 2 + ts // 4
                        v.wait(("p6_sg", u))
                        if ts % 4 == 0 and sid >= 2:
                            v.wait(("p6_hsp", sid - 2))
                        v.scalar_tensor_tensor(hst[:, sid % 2, ts % 4, :],
                                               PS[:, (4 + b4) * 512:(4 + b4) * 512 + 256],
                                               alpha[:, ts, e:e + 1], sgb[:, u % 2, :],
                                               ALU.mult, ALU.mult).inc("vq", 1, ("p6_h", u))
                        cpump(4)
                for dt in range(ND):
                    l = path * ND + dt
                    bb = dt % 2
                    v.wait(("p6_y_ps", l))
                    if hf == 1 and e == 0 and dt == 0:
                        v.wait(("p6_ysp", 0))
                        v.wait(("p4_mm", 127))
                    if e == 0:
                        v.tensor_copy(ycm[:, dt, :], PS[:, bb * 1024:bb * 1024 + 1024]).inc(
                            "vq", 1, ("p6_y", l))
                    else:
                        v.tensor_tensor(ycm[:, dt, :], ycm[:, dt, :],
                                        PS[:, bb * 1024:bb * 1024 + 1024], ALU.add).inc(
                            "vq", 1, ("p6_y", l))
                    cpump(4)
                cpump(len(cl))
            v.wait(("p6_y", (base + NPATH) * ND - 1))
            v.tensor_copy(smx[:, 0, 0:1], smx[:, 0, 0:1]).inc("vq", 1, ("p6_ylast", hf))
            if hf == 1:
                for j in range(TSH):
                    jj = TSH + j
                    v.wait(("p7_y", jj))
                    v.wait(("p7_x1", jj))
                    v.tensor_tensor(xtl[:, j % 2, :], xtl[:, j % 2, :], ytl[:, j % 2, :],
                                    ALU.add).inc("vq", 1, ("p7_add", jj))

    def emit_tensor(t):
        t.wait("p0_ones")
        for bi in range(4):
            for c in range(4):
                t.wait(("p0_brow", (bi, c)))
                if bi > 0:
                    t.wait(("p0_bias", (bi - 1, c)))
                t.matmul(PS[:, c * 512:(c + 1) * 512], ones1, brow,
                         start=True, stop=True).inc("pq", 1, ("p0_bias_ps", (bi, c)))
        for q8 in range(8):
            for m in range(3):
                t.wait(("p2_slab", (q8, m)))
            for ts in range(8):
                b = ts % 2
                bidx = q8 * 8 + ts
                if bidx < 2:
                    t.wait(("p0_bias", (3, 3)))
                    t.wait(("p2_aff", ND - 1))
                else:
                    for m in range(3):
                        t.wait(("p2_evac", ((bidx - 2) // 8, (bidx - 2) % 8, m)))
                for kb in range(ND):
                    for m in range(3):
                        t.matmul(PS[:, (m * 2 + b) * 512:(m * 2 + b) * 512 + 256],
                                 xnT[:, kb, ts * 128:(ts + 1) * 128],
                                 wqkv[:, q8 % 2, m, kb, :],
                                 start=(kb == 0), stop=(kb == ND - 1))
                t.inc("pq", 1, ("p2_mm", bidx))
        for q16 in range(16):
            for m in range(3):
                t.wait(("p2_slab2", (q16, m)))
            for ts in range(8, 16):
                b = ts % 2
                bidx2 = q16 * 8 + (ts - 8)
                if bidx2 < 2:
                    for m in range(3):
                        t.wait(("p2_evac", (7, 7, m)))
                else:
                    for m in range(3):
                        t.wait(("p2_evac2", ((bidx2 - 2) // 8, (bidx2 - 2) % 8 + 8, m)))
                for kb in range(ND):
                    for m in range(3):
                        t.matmul(PS[:, (m * 2 + b) * 512:(m * 2 + b) * 512 + 128],
                                 xnT[:, kb, ts * 128:(ts + 1) * 128],
                                 wq2[:, q16 % 2, m, kb, :],
                                 start=(kb == 0), stop=(kb == ND - 1))
                t.inc("pq", 1, ("p2_mm2", bidx2))
        for hf in range(NHALF):
            base = hf * NPATH
            for g in range(4):
                t.wait(("p4_slab", (hf, g * 2)))
                t.wait(("p4_slab", (hf, g * 2 + 1)))
                for tsl in range(8):
                    for jq2 in range(2):
                        idx = hf * 64 + g * 16 + tsl * 2 + jq2
                        q8 = g * 2 + jq2
                        bb = idx % 4
                        if idx < hf * 64 + 4:
                            if hf == 0:
                                for m in range(3):
                                    t.wait(("p2_evac2", (15, 15, m)))
                            else:
                                t.wait(("p6_y", NPATH * ND - 1))
                            for kb in range(ND):
                                t.wait(("p4_ctxT", (hf, kb)))
                        else:
                            t.wait(("p4_evac", idx - 4))
                        for kb in range(ND):
                            t.matmul(PS[:, bb * 512:bb * 512 + 256],
                                     ctxTh[:, kb, tsl * 128:(tsl + 1) * 128],
                                     wo_s[:, q8, kb, :],
                                     start=(kb == 0), stop=(kb == ND - 1))
                        t.inc("pq", 1, ("p4_mm", idx))
            for ts in range(TSH):
                if ts == 0:
                    t.wait(("p6_x1nT", (hf, ND - 1)))
                    if hf == 0:
                        t.wait("p0_rw")
                if ts >= 2:
                    t.wait(("p6_lg", (hf, ts - 2)))
                elif hf == 1:
                    t.wait(("p6_lg", (0, 6 + ts)))
                for kb in range(ND):
                    t.matmul(PS[:, 3072 + (ts % 2) * 512:3072 + (ts % 2) * 512 + 8],
                             x1nT[:, kb, ts * 128:(ts + 1) * 128],
                             rw_s[:, kb, :],
                             start=(kb == 0), stop=(kb == ND - 1))
                t.inc("pq", 1, ("p6_lg_ps", (hf, ts)))
            for e in range(NPATH):
                path = base + e
                for f8 in range(8):
                    l8 = path * 8 + f8
                    t.wait(("p6_gu", l8))
                    for ts in range(TSH):
                        u = l8 * 8 + ts
                        b4 = u % 4
                        if path == 0 and u < 4:
                            t.wait(("p4_evac", 63))
                        elif path >= 1 and u < path * 64 + 4:
                            t.wait(("p6_y", path * ND - 2 + (1 if b4 >= 2 else 0)))
                            if path == NPATH:
                                t.wait(("p4_evac", 127))
                            if u >= 4:
                                t.wait(("p6_h", u - 4))
                        else:
                            t.wait(("p6_h", u - 4))
                        if path == base and u - path * 64 < 4:
                            t.wait(("p6_al", (hf, TSH - 1)))
                        for kb in range(ND):
                            t.matmul(PS[:, b4 * 512:b4 * 512 + 256],
                                     x1nT[:, kb, ts * 128:(ts + 1) * 128],
                                     gus[:, l8 % 2, 0, kb, :],
                                     start=(kb == 0), stop=(kb == ND - 1))
                            t.matmul(PS[:, (4 + b4) * 512:(4 + b4) * 512 + 256],
                                     x1nT[:, kb, ts * 128:(ts + 1) * 128],
                                     gus[:, l8 % 2, 1, kb, :],
                                     start=(kb == 0), stop=(kb == ND - 1))
                        t.inc("pq", 1, ("p6_gu_ps", u))
                for dt in range(ND):
                    l = path * ND + dt
                    bb = dt % 2
                    t.wait(("p6_dsl", l))
                    if dt == 0:
                        t.wait(("p6_h", path * 64 + 63))
                    if dt >= 2:
                        t.wait(("p6_y", l - 2))
                    for f in range(16):
                        if dt == 0:
                            t.wait(("p6_HT", (hf, e, f)))
                        t.matmul(PS[:, bb * 1024:bb * 1024 + 512],
                                 dsl[:, l % 4, f, :], HT[:, f, 0:512],
                                 start=(f == 0), stop=(f == 15))
                        t.matmul(PS[:, bb * 1024 + 512:bb * 1024 + 1024],
                                 dsl[:, l % 4, f, :], HT[:, f, 512:1024],
                                 start=(f == 0), stop=(f == 15))
                    t.inc("pq", 1, ("p6_y_ps", l))

    emitters = {"sync": emit_sync, "gpsimd": emit_gpsimd, "scalar": emit_scalar,
                "vector": emit_vector, "tensor": emit_tensor}
    for n, f in emitters.items():
        f(Eng(plan, sems))
    plan.recording = False

    with nc.Block() as block:
        block.sync(lambda e: emit_sync(Eng(plan, sems, e)))
        block.gpsimd(lambda e: emit_gpsimd(Eng(plan, sems, e)))
        block.scalar(lambda e: emit_scalar(Eng(plan, sems, e)))
        block.vector(lambda e: emit_vector(Eng(plan, sems, e)))
        block.tensor(lambda e: emit_tensor(Eng(plan, sems, e)))

    st.close()
    return nc


_NC_CACHE = {}


def kernel(**inputs):
    if "main" not in _NC_CACHE:
        _NC_CACHE["main"] = build_nc()
    nc = _NC_CACHE["main"]
    xs = np.ascontiguousarray(np.asarray(inputs["L0_attn_x"], dtype=np.float32))
    names = {"ln_g": "L0_attn_ln_g", "ln_b": "L0_attn_ln_b", "wq": "L0_attn_Wq",
             "wk": "L0_attn_Wk", "wv": "L0_attn_Wv", "wo": "L0_attn_Wo",
             "bq": "L0_attn_bq", "bk": "L0_attn_bk", "bv": "L0_attn_bv",
             "bo": "L0_attn_bo", "rw": "L0_ffn_router_w", "gw": "L0_ffn_gate_w",
             "uw": "L0_ffn_up_w", "dw": "L0_ffn_down_w",
             "sgw": "L0_ffn_shared_gate_w", "suw": "L0_ffn_shared_up_w",
             "sdw": "L0_ffn_shared_down_w"}
    base = {k: np.ascontiguousarray(np.asarray(inputs[s], np.float32))
            for k, s in names.items()}
    in_maps = [dict(base, x=xs[b]) for b in range(8)]
    res = run_bass_kernel_spmd(nc, in_maps, core_ids=list(range(8)))
    return np.stack([r["out"] for r in res.results], axis=0)
